# version 05 (final)
# baseline (speedup 1.0000x reference)
import math
from contextlib import ExitStack

import numpy as np
import ml_dtypes
import concourse.bass as bass
import concourse.mybir as mybir
from concourse.bass_utils import run_bass_kernel_spmd

F32 = mybir.dt.float32
BF16 = mybir.dt.bfloat16
I32 = mybir.dt.int32
U32 = mybir.dt.uint32
ALU = mybir.AluOpType
AF = mybir.ActivationFunctionType

B, S, D = 2, 16384, 1024
NBLK, TPB = 128, 128
LOCBLK, TOK = 32, 4096
DC = 8
DFF = 4096
NST = 16
MU = math.sqrt(0.5)
SDEN = math.sqrt(0.25 * math.pi) * math.sqrt(2.0)
SPRIME = 1.0 / (S * SDEN)
FOFF = 896
FLEN = 17408
EPS_LN = 1e-5
EPS_RMS = 1e-6
NSPAN, SPW = 8, 512
RG = [[0, 1, 2, 3], [4, 5, 6, 7]]

def _owner(b):
    return b % 4 if b < 64 else (127 - b) % 4

BLOCKS = {rg: [b for b in range(NBLK) if _owner(b) == rg] for rg in range(4)}
RANKOF = [_owner(b) for b in range(NBLK)]
SLOTOF = [0] * NBLK
for _rg in range(4):
    for _i, _b in enumerate(BLOCKS[_rg]):
        SLOTOF[_b] = _i
SH_LO = [0, 1, 2, 3]
SH_HI = [3, 2, 1, 0]
KREG = 128 * TOK
VREG = TOK * D


def _bucket_of(delta):
    NB, MAXD = 32, 128
    if delta < NB // 2:
        return delta
    nf = np.float32(max(delta, 1))
    v = NB // 2 + int(np.float32(np.log(nf / (NB // 2)) / math.log(MAXD / (NB // 2)) * (NB - NB // 2)))
    return min(v, NB - 1)


_HOSTC = {}

def _host_constants():
    if _HOSTC:
        return _HOSTC
    buckets = np.array([_bucket_of(d) for d in range(S)], dtype=np.int64)
    ohts, madds = [], []
    for sh in range(4):
        k = np.arange(FLEN)
        delta = k - FOFF + 128 * sh
        valid = (delta >= 0) & (delta < S)
        oht = np.zeros((32, FLEN), dtype=np.float32)
        dv = delta[valid]
        oht[buckets[dv], k[valid]] = 1.0
        madd = np.where(valid, np.float32(-MU / SDEN), np.float32(-1e9))[None, :].astype(np.float32)
        ohts.append(np.ascontiguousarray(oht))
        madds.append(np.ascontiguousarray(madd))
    dt = np.exp(np.linspace(math.log(0.001), math.log(0.1), NST)).astype(np.float32)
    _HOSTC.update(
        ohts=ohts, madds=madds,
        dtneg=np.tile(-dt, (128, 2)).astype(np.float32),
        idm=np.eye(128, dtype=np.float32).astype(ml_dtypes.bfloat16),
        idmf=np.eye(128, dtype=np.float32),
        ones_row_b=np.ones((1, 128), dtype=np.float32).astype(ml_dtypes.bfloat16),
        sel8=np.concatenate([np.tile((np.arange(8) == e).astype(np.float32)[:, None], (1, 128)) for e in range(8)], axis=1).astype(ml_dtypes.bfloat16),
        ones_col_b=np.ones((128, 1), dtype=np.float32).astype(ml_dtypes.bfloat16),
        ones_row_f=np.ones((1, 128), dtype=np.float32),
    )
    return _HOSTC


class Ctr:
    __slots__ = ("h", "n")
    def __init__(self, h):
        self.h = h
        self.n = 0
    def inc(self, instr, k=1):
        instr.then_inc(self.h, k)
        self.n += k
        return instr


class _Done(Exception):
    pass


def ap_of(t, off, dims):
    tt = t.tensor if isinstance(t, bass.AP) else t
    return bass.AP(tt, off, dims)


def build_nc(upto=4):
    nc = bass.Bass("TRN2", target_bir_lowering=False)
    try:
        _build_body(nc, upto)
    except _Done:
        pass
    return nc


def _build_body(nc, upto):
    st = ExitStack()

    def I(name, shape, dtype=F32):
        return nc.dram_tensor(name, shape, dtype, kind="ExternalInput").ap()

    x_in = I("x_fm", [D, TOK])
    wqk_in = I("L0_attn_a_Wqk", [D, 128])
    bqk_in = I("L0_attn_a_bqk", [128])
    gam_in = I("L0_attn_a_gamma", [2, 128])
    bet_in = I("L0_attn_a_beta", [2, 128])
    wv_in = I("L0_attn_a_Wv", [D, D])
    bv_in = I("L0_attn_a_bv", [D])
    rel_in = I("L0_attn_a_rel_emb", [32, 1])
    oht_lo_in = I("oht_lo", [32, FLEN])
    oht_hi_in = I("oht_hi", [32, FLEN])
    madd_lo_in = I("madd_lo", [1, FLEN])
    madd_hi_in = I("madd_hi", [1, FLEN])
    am_in = I("A_mine", [128, 32])
    bm_in = I("B_mine", [128, 32])
    cm_in = I("C_mine", [128, 32])
    d_in = I("D_cols", [128, DC])
    s1_in = I("s1_cols", [128, DC])
    s2_in = I("s2_cols", [128, DC])
    b2_in = I("b2_cols", [128, DC])
    b1_in = I("b1_cols", [128, 32])
    w1_in = I("L0_attn_b_w1", [D, DFF])
    w2_in = I("L0_attn_b_w2", [DFF, D])
    rwt_in = I("rwT", [2, D, 8])
    rb_in = I("L0_ffn_router_b", [2, 8])
    ewt_in = I("ewT", [2, 8, D, D])
    eb_in = I("L0_ffn_expert_b", [2, 8, D])
    dtneg_in = I("dtneg", [128, 32])
    idm_in = I("idm", [128, 128], BF16)
    idmf_in = I("idmf", [128, 128])
    onesrb_in = I("ones_row_b", [1, 128], BF16)
    onesrf_in = I("ones_row_f", [1, 128])
    onescb_in = I("ones_col_b", [128, 1], BF16)
    sel8_in = I("sel8", [8, 8 * 128], BF16)
    rgv_in = I("rgv", [1, 1], I32)
    rbrep_in = I("rb_rep", [128, 16])
    two_in = I("two_rep", [128, 8])
    rwth_in = I("rwT_hi", [2, D, 128], BF16)
    rwtl_in = I("rwT_lo", [2, D, 128], BF16)

    y_out = nc.dram_tensor("y_fm", [D, TOK], F32, kind="ExternalOutput").ap()

    def DSr(name, shape, dtype=F32):
        return nc.dram_tensor(name, shape, dtype, kind="Internal").ap()

    stream_d = DSr("stream", [D, TOK])
    u_d = DSr("u_buf", [D, TOK])
    lnbf_d = DSr("lnbf", [D, TOK], BF16)
    lnbf2_d = DSr("lnbf2", [D, TOK], BF16)
    f_lo_d = DSr("f_lo", [FLEN], BF16)
    f_hi_d = DSr("f_hi", [FLEN], BF16)
    kvin_d = DSr("kvin", [KREG + VREG], BF16)
    kvout_d = DSr("kvout", [4 * (KREG + VREG)], BF16)
    urin_d = DSr("urin", [D, TOK])
    urout_d = DSr("urout", [4 * D, TOK])
    yin_d = DSr("yin", [256, S])
    yout_d = DSr("yout", [4 * 256, S])
    urmine_d = DSr("urmine", [4 * 256 * TOK])
    ymine_d = DSr("ymine", [D, TOK])
    o1f32_d = DSr("o1f32", [D, TOK])
    mb_d = DSr("mb_bounce", [128, 8])
    ewtb_d = DSr("ewTb", [2 * 8 * D * D], BF16)
    w1b_d = DSr("w1b", [D * DFF], BF16)
    w2b_d = DSr("w2b", [DFF * D], BF16)

    cs = {}
    for name in ["dS", "dG", "dC", "cT", "cV", "cS", "cc", "dO", "bar"]:
        cs[name] = Ctr(st.enter_context(nc.semaphore(name)))
    dS, dG, dC, cT, cV, cS_, cc, dO, barS = (cs[k] for k in
        ["dS", "dG", "dC", "cT", "cV", "cS", "cc", "dO", "bar"])

    block = st.enter_context(nc.Block())

    class _P:
        __slots__ = ("_e", "_last")
        def __init__(self, eng):
            self._e = eng
            self._last = None
        def __getattr__(self, name):
            a = getattr(self._e, name)
            if callable(a):
                def w(*args, **kw):
                    r = a(*args, **kw)
                    if hasattr(r, "then_inc"):
                        self._last = r
                    return r
                return w
            return a

    def on(engname, f):
        getattr(block, engname)(lambda e: f(_P(e)))

    def fin(e, ctr, k=1):
        last = e._last if e._last is not None else e._e.nop()
        ctr.inc(last, k)

    ENGS = ["sync", "tensor", "vector", "scalar", "gpsimd"]

    def barrier():
        def _ann(e):
            barS.inc(e._e.nop() if hasattr(e, "_e") else e.nop(), 1)
        for en in ENGS:
            on(en, _ann)
        tgt = [(c, c.n) for c in [barS, dS, dG, dC, cT, cV, cS_, cc, dO] if c.n > 0]
        def _wait(e):
            for c, v in tgt:
                e.wait_ge(c.h, v)
        for en in ENGS:
            on(en, _wait)

    gb_s = st.enter_context(nc.sbuf_tensor("gb_s", [128, 8], F32))
    rel_s = st.enter_context(nc.sbuf_tensor("rel_s", [32, 1], F32))
    ssm_s = st.enter_context(nc.sbuf_tensor("ssm_s", [128, 256], F32))
    rwt_s = st.enter_context(nc.sbuf_tensor("rwt_s", [128, 128], F32))
    rbrow_s = st.enter_context(nc.sbuf_tensor("rbrow_s", [1, 16], F32))
    ebs_s = st.enter_context(nc.sbuf_tensor("ebs_s", [8, 2 * D], BF16))
    idm_s = st.enter_context(nc.sbuf_tensor("idm_s", [128, 128], BF16))
    idmf_s = st.enter_context(nc.sbuf_tensor("idmf_s", [128, 128], F32))
    onesrb_s = st.enter_context(nc.sbuf_tensor("onesrb_s", [1, 128], BF16))
    onesrf_s = st.enter_context(nc.sbuf_tensor("onesrf_s", [1, 128], F32))
    onescb_s = st.enter_context(nc.sbuf_tensor("onescb_s", [128, 1], BF16))
    rgs_s = st.enter_context(nc.sbuf_tensor("rgs_s", [1, 1], I32))
    stats_s = st.enter_context(nc.sbuf_tensor("stats_s", [1, 6 * SPW], F32))
    maskT_s = st.enter_context(nc.sbuf_tensor("maskT_s", [8, TOK], BF16))
    sel8_s = st.enter_context(nc.sbuf_tensor("sel8_s", [8, 8 * 128], BF16))
    rbrep_s = st.enter_context(nc.sbuf_tensor("rbrep_s", [128, 16], F32))
    rwth_s = st.enter_context(nc.sbuf_tensor("rwth_s", [128, 1024], BF16))
    rwtl_s = st.enter_context(nc.sbuf_tensor("rwtl_s", [128, 1024], BF16))

    def _p0_sync(e):
        dd = lambda o, i: dS.inc(e.dma_start(out=o, in_=i), 16)
        dd(rel_s[:], rel_in[:, :])
        dd(gb_s[:, 0:1], gam_in[0:1, :].transpose([1, 0]))
        dd(gb_s[:, 1:2], bet_in[0:1, :].transpose([1, 0]))
        dd(gb_s[:, 2:3], gam_in[1:2, :].transpose([1, 0]))
        dd(gb_s[:, 3:4], bet_in[1:2, :].transpose([1, 0]))
        dd(gb_s[:, 4:5], ap_of(bqk_in, 0, [[1, 128], [1, 1]]))
        dd(idm_s[:], idm_in[:, :])
        dd(idmf_s[:], idmf_in[:, :])
        dd(onesrb_s[:], onesrb_in[:, :])
        dd(onesrf_s[:], onesrf_in[:, :])
        dd(onescb_s[:], onescb_in[:, :])
        dd(rgs_s[:], rgv_in[:, :])
        dd(ssm_s[:, 128:160], am_in[:, :])
        dd(ssm_s[:, 160:192], bm_in[:, :])
        dd(ssm_s[:, 192:224], cm_in[:, :])
        dd(ssm_s[:, 224:256], dtneg_in[:, :])
        dd(ssm_s[:, 64:72], d_in[:, :])
        dd(ssm_s[:, 72:80], s1_in[:, :])
        dd(ssm_s[:, 80:88], s2_in[:, :])
        dd(ssm_s[:, 88:96], b2_in[:, :])
        dd(ssm_s[:, 96:128], b1_in[:, :])
        dd(rwt_s[:, :], ap_of(rwt_in, 0, [[8, 128], [8192, 2], [1024, 8], [1, 8]]))
        dd(rbrow_s[:], ap_of(rb_in, 0, [[16, 1], [1, 16]]))
        dd(sel8_s[:], sel8_in[:, :])
        dd(rbrep_s[:], rbrep_in[:, :])

    on("sync", _p0_sync)
    P0_SMALL = dS.n

    def _p0_gp(e):
        dG.inc(e.dma_start(out=ebs_s[:], in_=ap_of(eb_in, 0, [[1024, 8], [8 * 1024, 2], [1, 1024]])), 16)
    on("gpsimd", _p0_gp)

    def _v1(e):
        e.wait_ge(dS.h, P0_SMALL)
        e.tensor_scalar(out=gb_s[:, 5:6], in0=gb_s[:, 0:1], scalar1=SPRIME, scalar2=None, op0=ALU.mult)
        e.tensor_scalar(out=gb_s[:, 6:7], in0=gb_s[:, 1:2], scalar1=SPRIME, scalar2=None, op0=ALU.mult)
        e.tensor_scalar(out=rel_s[:, 0:1], in0=rel_s[:, 0:1],
                        scalar1=float(math.sqrt(128.0) / SDEN), scalar2=None, op0=ALU.mult)
        fin(e, cV)
    on("vector", _v1)

    def _s1(e):
        e.wait_ge(dS.h, P0_SMALL)
        e.activation(out=ssm_s[:, 128:160], in_=ssm_s[:, 128:160], func=AF.Exp)
        fin(e, cS_)
    on("scalar", _s1)

    def _v2(e):
        e.wait_ge(cS_.h, cS_.n)
        cV.inc(e.tensor_tensor(out=ssm_s[:, 0:32], in0=ssm_s[:, 128:160], in1=ssm_s[:, 224:256], op=ALU.mult))
    on("vector", _v2)

    def _s2(e):
        e.wait_ge(cV.h, cV.n)
        e.activation(out=ssm_s[:, 0:32], in_=ssm_s[:, 0:32], func=AF.Exp)
        fin(e, cS_)
    on("scalar", _s2)

    def _v3(e):
        e.wait_ge(cS_.h, cS_.n)
        cV.inc(e.reciprocal(out=ssm_s[:, 224:256], in_=ssm_s[:, 128:160]))
        e.wait_ge(cV.h, cV.n)
        cV.inc(e.tensor_scalar(out=ssm_s[:, 32:64], in0=ssm_s[:, 0:32], scalar1=-1.0, scalar2=1.0, op0=ALU.mult, op1=ALU.add))
        e.wait_ge(cV.h, cV.n)
        cV.inc(e.tensor_tensor(out=ssm_s[:, 32:64], in0=ssm_s[:, 32:64], in1=ssm_s[:, 160:192], op=ALU.mult))
        e.wait_ge(cV.h, cV.n)
        cV.inc(e.tensor_tensor(out=ssm_s[:, 32:64], in0=ssm_s[:, 32:64], in1=ssm_s[:, 224:256], op=ALU.mult))
        e.wait_ge(cV.h, cV.n)
        cV.inc(e.tensor_tensor(out=ssm_s[:, 32:64], in0=ssm_s[:, 32:64], in1=ssm_s[:, 192:224], op=ALU.mult))
        e.wait_ge(cV.h, cV.n)
        e.tensor_scalar(out=ssm_s[:, 64:72], in0=ssm_s[:, 64:72], scalar1=1.0, scalar2=None, op0=ALU.add)
        fin(e, cV)
    on("vector", _v3)

    with nc.sbuf_tensor("ohtc_s", [32, FLEN], F32) as ohtc_s, \
         nc.sbuf_tensor("madd_s", [1, FLEN], F32) as madd_s, \
         nc.sbuf_tensor("frow_s", [1, FLEN], BF16) as frow_s, \
         nc.psum_tensor("p0f", [1, 1024], F32) as p0f:
        NCH = FLEN // 512
        for oin, min_, fd in [(oht_lo_in, madd_lo_in, f_lo_d), (oht_hi_in, madd_hi_in, f_hi_d)]:
            cvg = cV.n
            def _ld(e, oin=oin, min_=min_, cvg=cvg):
                e.wait_ge(cV.h, cvg)
                dS.inc(e.dma_start(out=ohtc_s[:, :], in_=oin[:, :]), 16)
                dS.inc(e.dma_start(out=madd_s[:, :], in_=min_[:, :]), 16)
            on("sync", _ld)
            dw = dS.n
            ct0 = cT.n
            cv0 = cV.n
            def _mm(e, dw=dw, cv0=cv0):
                e.wait_ge(dS.h, dw)
                e.wait_ge(cV.h, cv0)
                for ch in range(NCH):
                    if ch >= 2:
                        e.wait_ge(cV.h, cv0 + ch - 1)
                    cT.inc(e.matmul(p0f[0:1, (ch % 2) * 512:(ch % 2) * 512 + 512], rel_s[:, 0:1],
                                    ohtc_s[:, ch * 512:(ch + 1) * 512], start=True, stop=True))
            on("tensor", _mm)
            def _ev(e, ct0=ct0):
                for ch in range(NCH):
                    e.wait_ge(cT.h, ct0 + ch + 1)
                    cV.inc(e.tensor_tensor(out=frow_s[0:1, ch * 512:(ch + 1) * 512],
                                           in0=p0f[0:1, (ch % 2) * 512:(ch % 2) * 512 + 512],
                                           in1=madd_s[0:1, ch * 512:(ch + 1) * 512], op=ALU.add))
            on("vector", _ev)
            def _stf(e, fd=fd, cvn=cV.n):
                e.wait_ge(cV.h, cvn)
                dS.inc(e.dma_start(out=ap_of(fd, 0, [[FLEN, 1], [1, FLEN]]), in_=frow_s[0:1, :]), 16)
            on("sync", _stf)
        barrier()

    CASTS_DONE = dG.n

    SL = lambda i: stats_s[0:1, i * SPW:(i + 1) * SPW]

    def emit_dance(xs_get, want_rr, eps_rr, pS1, pS2, pR, pMR, pRR,
                   xb_s, sq_s, apply_fn, rr_apply_fn=None, pre=(), skip_mean=False):
        def _sc(e, _ctp=None):
            for c, v in pre:
                e.wait_ge(c.h, v)
            if _ctp is None:
                _ctp = _sc._ctp
            e.wait_ge(cT.h, _ctp)
            for dc in range(DC):
                if not skip_mean:
                    e.activation(out=xb_s[:, dc * SPW:(dc + 1) * SPW], in_=xs_get(dc), func=AF.Copy)
                e.activation(out=sq_s[:, dc * SPW:(dc + 1) * SPW], in_=xs_get(dc), func=AF.Square)
            fin(e, cS_)
        _sc._ctp = cT.n
        on("scalar", _sc)
        cs0 = cS_.n
        cvp = cV.n
        def _mm(e, cs0=cs0, cvp=cvp):
            e.wait_ge(cS_.h, cs0)
            e.wait_ge(cV.h, cvp)
            if not skip_mean:
                for dc in range(DC):
                    e.matmul(pS1, onescb_s[:, :], xb_s[:, dc * SPW:(dc + 1) * SPW],
                             start=(dc == 0), stop=(dc == DC - 1))
            for dc in range(DC):
                e.matmul(pS2, onescb_s[:, :], sq_s[:, dc * SPW:(dc + 1) * SPW],
                         start=(dc == 0), stop=(dc == DC - 1))
            fin(e, cT)
        on("tensor", _mm)
        ct0 = cT.n
        def _st1(e, ct0=ct0):
            e.wait_ge(cT.h, ct0)
            if skip_mean:
                e.tensor_scalar(out=SL(3), in0=pS2, scalar1=1.0 / D, scalar2=eps_rr, op0=ALU.mult, op1=ALU.add)
            else:
                e.tensor_scalar(out=SL(0), in0=pS1, scalar1=1.0 / D, scalar2=None, op0=ALU.mult)
                e.tensor_scalar(out=SL(3), in0=pS2, scalar1=1.0 / D, scalar2=None, op0=ALU.mult)
                e.tensor_tensor(out=SL(4), in0=SL(0), in1=SL(0), op=ALU.mult)
                e.tensor_tensor(out=SL(3), in0=SL(3), in1=SL(4), op=ALU.subtract)
                e.tensor_scalar(out=SL(3), in0=SL(3), scalar1=EPS_LN, scalar2=None, op0=ALU.add)
            cV.inc(e._last if e._last is not None else e._e.nop())
            e.wait_ge(cV.h, cV.n)
            e.reciprocal(out=SL(5), in_=SL(3))
            fin(e, cV)
        on("vector", _st1)
        def _st2(e, cvn=cV.n):
            e.wait_ge(cV.h, cvn)
            e.activation(out=SL(1), in_=SL(5), func=AF.Sqrt)
            fin(e, cS_)
        on("scalar", _st2)
        def _st3(e, csn=cS_.n):
            e.wait_ge(cS_.h, csn)
            e.tensor_tensor(out=SL(4), in0=SL(1), in1=SL(1), op=ALU.mult)
            e.tensor_tensor(out=SL(4), in0=SL(4), in1=SL(3), op=ALU.mult)
            e.tensor_scalar(out=SL(4), in0=SL(4), scalar1=-0.5, scalar2=1.5, op0=ALU.mult, op1=ALU.add)
            e.tensor_tensor(out=SL(1), in0=SL(1), in1=SL(4), op=ALU.mult)
            if not skip_mean:
                e.tensor_tensor(out=SL(2), in0=SL(0), in1=SL(1), op=ALU.mult)
            fin(e, cV)
        on("vector", _st3)
        if want_rr:
            def _st4(e):
                e.tensor_scalar(out=SL(3), in0=SL(5), scalar1=-EPS_LN, scalar2=1.0 + eps_rr,
                                op0=ALU.mult, op1=ALU.add)
                cV.inc(e._last)
                e.wait_ge(cV.h, cV.n)
                e.reciprocal(out=SL(5), in_=SL(3))
                fin(e, cV)
            on("vector", _st4)
            def _st5(e, cvn=cV.n):
                e.wait_ge(cV.h, cvn)
                e.activation(out=SL(4), in_=SL(5), func=AF.Sqrt)
                fin(e, cS_)
            on("scalar", _st5)
            def _st6(e, csn=cS_.n):
                e.wait_ge(cS_.h, csn)
                e.tensor_tensor(out=SL(5), in0=SL(4), in1=SL(4), op=ALU.mult)
                e.tensor_tensor(out=SL(5), in0=SL(5), in1=SL(3), op=ALU.mult)
                e.tensor_scalar(out=SL(5), in0=SL(5), scalar1=-0.5, scalar2=1.5, op0=ALU.mult, op1=ALU.add)
                e.tensor_tensor(out=SL(5), in0=SL(4), in1=SL(5), op=ALU.mult)
                fin(e, cV)
            on("vector", _st6)
        cvr = cV.n
        def _r1(e, cvr=cvr):
            e.wait_ge(cV.h, cvr)
            if not skip_mean:
                e.matmul(pR, onesrf_s[:, :], SL(1), start=True, stop=True)
                e.matmul(pMR, onesrf_s[:, :], SL(2), start=True, stop=True)
            else:
                e.matmul(pR, onesrf_s[:, :], SL(1), start=True, stop=True)
            if want_rr:
                e.matmul(pRR, onesrf_s[:, :], SL(5), start=True, stop=True)
            fin(e, cT)
        on("tensor", _r1)
        def _ap(e, ctn=cT.n):
            e.wait_ge(cT.h, ctn)
            for dc in range(DC):
                apply_fn(e, dc)
            if rr_apply_fn is not None:
                for dc in range(DC):
                    rr_apply_fn(e, dc)
            fin(e, cV)
        on("vector", _ap)

    if upto == 0:
        def _cp0(e):
            for dc in range(DC):
                dO.inc(e.dma_start(out=ap_of(y_out, dc * 128 * TOK, [[TOK, 128], [1, TOK]]),
                                   in_=ap_of(x_in, dc * 128 * TOK, [[TOK, 128], [1, TOK]])), 16)
        on("gpsimd", _cp0)
        barrier()
        raise _Done

    with nc.sbuf_tensor("wqk_s", [128, 8 * 128], BF16) as wqk_s, \
         nc.sbuf_tensor("wv_s", [128, 8 * 1024], BF16) as wv_s, \
         nc.sbuf_tensor("bvrow_s", [1, D], F32) as bvrow_s, \
         nc.sbuf_tensor("qb16_s", [128, TOK], BF16) as qb16_s:

        def _p1w(e):
            dG.inc(e.dma_start(out=wqk_s[:], in_=ap_of(wqk_in, 0, [[128, 128], [128 * 128, 8], [1, 128]])), 16)
            dG.inc(e.dma_start(out=wv_s[:], in_=ap_of(wv_in, 0, [[1024, 128], [128 * 1024, 8], [1, 1024]])), 16)
        on("gpsimd", _p1w)
        def _p1bv(e):
            dS.inc(e.dma_start(out=bvrow_s[:], in_=ap_of(bv_in, 0, [[D, 1], [1, D]])), 16)
        on("sync", _p1bv)
        P1W = (dG.n, dS.n)

        with nc.sbuf_tensor("xs_s", [128, DC * SPW], F32) as xs_s, \
             nc.sbuf_tensor("xb_s", [128, DC * SPW], BF16) as xb_s, \
             nc.sbuf_tensor("sq_s", [128, DC * SPW], BF16) as sq_s, \
             nc.sbuf_tensor("lnrev_s", [128, DC * SPW], BF16) as lnrev_s, \
             nc.sbuf_tensor("t1_s", [128, SPW], F32) as t1_s, \
             nc.sbuf_tensor("qks_s", [128, SPW], F32) as qks_s, \
             nc.sbuf_tensor("kb16_s", [128, TOK], BF16) as kb16_s, \
             nc.sbuf_tensor("vb16_s", [128, 1024], BF16) as vb16_s, \
             nc.psum_tensor("pP1", [128, 4096], F32) as pP1:
            pS1 = pP1[0:1, 0:SPW]
            pS2 = pP1[0:1, 512:512 + SPW]
            pR = pP1[:, 1024:1536]
            pMR = pP1[:, 1536:2048]
            pQK = pP1[:, 2048:2560]
            pV = pP1[:, 2560:3584]

            for sp in range(NSPAN):
                def _ldx(e, sp=sp, csn=cS_.n, cvn=cV.n):
                    e.wait_ge(cS_.h, csn)
                    e.wait_ge(cV.h, cvn)
                    dS.inc(e.dma_start(out=xs_s[:, :], in_=ap_of(x_in, sp * SPW,
                           [[TOK, 128], [128 * TOK, DC], [1, SPW]])), 16)
                on("sync", _ldx)
                dxw = dS.n

                def lnrev_apply(e, dc, sp=sp):
                    e.tensor_tensor(out=t1_s[:, :], in0=xs_s[:, dc * SPW:(dc + 1) * SPW], in1=pR, op=ALU.mult)
                    outap = ap_of(lnrev_s, dc * SPW + 127, [[DC * SPW, 128], [128, 4], [-1, 128]])
                    e.tensor_tensor(out=outap, in0=t1_s[:, :], in1=pMR, op=ALU.subtract)

                ctw = cT.n
                def lnrev_apply2(e, dc, ctw=ctw):
                    if dc == 0:
                        e.wait_ge(cT.h, ctw)
                    lnrev_apply(e, dc)

                emit_dance(lambda dc: xs_s[:, dc * SPW:(dc + 1) * SPW], False, 0.0,
                           pS1, pS2, pR, pMR, None, xb_s, sq_s, lnrev_apply2,
                           pre=((dS, dxw),))
                LNREV = cV.n

                if upto == 11:
                    continue
                def _qk(e, lnd=LNREV, csp=cS_.n, p1w=P1W):
                    e.wait_ge(cV.h, lnd)
                    e.wait_ge(cS_.h, csp)
                    e.wait_ge(dG.h, p1w[0])
                    for dc in range(DC):
                        e.matmul(pQK, wqk_s[:, dc * 128:(dc + 1) * 128],
                                 lnrev_s[:, dc * SPW:(dc + 1) * SPW], start=(dc == 0), stop=(dc == DC - 1))
                    fin(e, cT)
                on("tensor", _qk)

                def _silu(e, ct0=cT.n):
                    e.wait_ge(cT.h, ct0)
                    e.activation(out=qks_s[:, :], in_=pQK, func=AF.Silu, bias=gb_s[:, 4:5], scale=1.0)
                    fin(e, cS_)
                on("scalar", _silu)

                def _kq(e, sp=sp, csn=cS_.n):
                    e.wait_ge(cS_.h, csn)
                    e.tensor_scalar(out=kb16_s[:, sp * SPW:(sp + 1) * SPW], in0=qks_s[:, :],
                                    scalar1=gb_s[:, 2:3], scalar2=gb_s[:, 3:4], op0=ALU.mult, op1=ALU.add)
                    qout = ap_of(qb16_s, sp * SPW + 127, [[TOK, 128], [128, 4], [-1, 128]])
                    e.tensor_scalar(out=qout, in0=qks_s[:, :],
                                    scalar1=gb_s[:, 5:6], scalar2=gb_s[:, 6:7], op0=ALU.mult, op1=ALU.add)
                    fin(e, cV)
                on("vector", _kq)

                for tb in (range(4) if upto != 12 else []):
                    slot = sp * 4 + tb
                    def _vmm(e, tb=tb, lnd=LNREV, csp=cS_.n, p1w=P1W):
                        e.wait_ge(cV.h, lnd)
                        e.wait_ge(cS_.h, csp)
                        e.wait_ge(dG.h, p1w[0])
                        e.wait_ge(dS.h, p1w[1])
                        for h in range(2):
                            for dc in range(DC):
                                e.matmul(pV[:, h * 512:(h + 1) * 512],
                                         lnrev_s[:, dc * SPW + tb * 128: dc * SPW + (tb + 1) * 128],
                                         wv_s[:, dc * 1024 + h * 512: dc * 1024 + (h + 1) * 512],
                                         start=(dc == 0), stop=False)
                            e.matmul(pV[:, h * 512:(h + 1) * 512], onesrf_s[:, :],
                                     bvrow_s[0:1, h * 512:(h + 1) * 512], start=False, stop=True)
                        fin(e, cT)
                    on("tensor", _vmm)

                    def _vsilu(e, ct0=cT.n):
                        e.wait_ge(cT.h, ct0)
                        e.activation(out=vb16_s[:, :], in_=pV, func=AF.Silu)
                        fin(e, cS_)
                    on("scalar", _vsilu)

                    def _vst(e, slot=slot, csn=cS_.n):
                        e.wait_ge(cS_.h, csn)
                        dS.inc(e.dma_start(out=ap_of(kvin_d, KREG + slot * 128 * 1024, [[1024, 128], [1, 1024]]),
                                           in_=vb16_s[:, :]), 16)
                    on("sync", _vst)

            if upto not in (11,):
                def _kst(e, cvn=cV.n):
                    e.wait_ge(cV.h, cvn)
                    dS.inc(e.dma_start(out=ap_of(kvin_d, 0, [[TOK, 128], [1, TOK]]), in_=kb16_s[:, :]), 16)
                on("sync", _kst)
            if upto not in (11, 12, 13):
                def _ag1(e, dsn=dS.n):
                    e.wait_ge(dS.h, dsn)
                    cc.inc(e.collective_compute("AllGather", ALU.bypass, replica_groups=RG,
                                                ins=[kvin_d], outs=[kvout_d]))
                on("gpsimd", _ag1)
            barrier()

        if upto in (10, 11, 12, 13):
            def _dbg1(e):
                dO.inc(e.dma_start(out=ap_of(y_out, 0, [[TOK, 128], [1, TOK]]), in_=qb16_s[:, :]), 16)
                dO.inc(e.dma_start(out=ap_of(y_out, 128 * TOK, [[TOK, 128], [1, TOK]]),
                                   in_=ap_of(kvin_d, 0, [[TOK, 128], [1, TOK]])), 16)
                dO.inc(e.dma_start(out=ap_of(y_out, 256 * TOK, [[TOK, 128], [1, TOK]]),
                                   in_=lnrev_s[:, :]), 16)
                dO.inc(e.dma_start(out=ap_of(y_out, 384 * TOK, [[TOK, 640], [1, TOK]]),
                                   in_=ap_of(kvin_d, KREG, [[TOK, 640], [1, TOK]])), 16)
            on("gpsimd", _dbg1)
            barrier()
            raise _Done

        with nc.sbuf_tensor("kfull_s", [128, S], BF16) as kfull_s, \
             nc.sbuf_tensor("vt_s", [128, 2 * 16 * 1024], BF16) as vt_s, \
             nc.sbuf_tensor("bq_s", [128, 2 * 512], BF16) as bq_s, \
             nc.sbuf_tensor("esb_s", [128, 512], F32) as esb_s, \
             nc.sbuf_tensor("ebf_s", [128, 512], BF16) as ebf_s, \
             nc.sbuf_tensor("ep_s", [128, 2 * 512], BF16) as ep_s, \
             nc.sbuf_tensor("acc_s", [128, 8 * 1024], F32) as acc_s, \
             nc.sbuf_tensor("xcols_s", [128, 8 * 128], F32) as xcols_s, \
             nc.sbuf_tensor("stile_s", [128, 2 * 128], F32) as stile_s, \
             nc.psum_tensor("pOut", [128, 2 * 1024], F32) as pOut, \
             nc.psum_tensor("pSim", [128, 2 * 512], F32) as pSim, \
             nc.psum_tensor("pT", [128, 2 * 512], F32) as pT:

            def _ldk(e, ccn=cc.n):
                e.wait_ge(cc.h, ccn)
                for r in range(4):
                    dS.inc(e.dma_start(out=kfull_s[:, r * TOK:(r + 1) * TOK],
                                       in_=ap_of(kvout_d, r * (KREG + VREG), [[TOK, 128], [1, TOK]])), 16)
            on("sync", _ldk)
            KLOAD = dS.n

            qd_i = [0]
            g_simTT = {}
            g_erf = {}
            g_aff = {}
            g_omm = {}
            vt_guard = {}
            pout_guard = {}
            pT_guard = {}
            acc_tr_guard = {}
            stile_guard = {}
            slot_JJcnt = [(4 * i + 3) // 16 + 1 for i in range(32)]

            for g in range(4):
                njj = 2 * g + 2
                for JJ in range(njj):
                    par = JJ % 2
                    def _ldv(e, JJ=JJ, par=par, guard=vt_guard.get(par)):
                        if guard is not None:
                            e.wait_ge(cT.h, guard)
                        for k in range(16):
                            jb = JJ * 16 + k
                            off = RANKOF[jb] * (KREG + VREG) + KREG + SLOTOF[jb] * 128 * 1024
                            dS.inc(e.dma_start(out=vt_s[:, (par * 16 + k) * 1024:(par * 16 + k + 1) * 1024],
                                               in_=ap_of(kvout_d, off, [[1024, 128], [1, 1024]])), 16)
                    on("sync", _ldv)
                    dS_v = dS.n

                    for s8 in range(8):
                        i = 8 * g + s8
                        if JJ >= slot_JJcnt[i]:
                            continue
                        quads = [qd for qd in range(4) if JJ * 16 + qd * 4 <= 4 * i + 3]
                        op = i % 2
                        first_jj = (JJ == 0)
                        last_jj = (JJ == slot_JJcnt[i] - 1)

                        for qn, qd in enumerate(quads):
                            q0 = JJ * 16 + qd * 4
                            qi = qd_i[0]; qd_i[0] += 1
                            bpar = qi % 2
                            fsrc = f_lo_d if i < 16 else f_hi_d
                            base = 512 * i - 128 * q0 + 385

                            def _ldb(e, base=base, fsrc=fsrc, bpar=bpar, gq=g_simTT.get(qi - 2)):
                                if gq is not None:
                                    e.wait_ge(cV.h, gq)
                                dS.inc(e.dma_start(out=bq_s[:, bpar * 512:(bpar + 1) * 512],
                                                   in_=ap_of(fsrc, base, [[1, 128], [128, 4], [1, 128]])), 16)
                            on("sync", _ldb)
                            dS_b = dS.n

                            def _sim(e, q0=q0, bpar=bpar, i=i, gq=g_simTT.get(qi - 2), kl=KLOAD, first=(qi == 0)):
                                if first:
                                    e.wait_ge(dS.h, kl)
                                if gq is not None:
                                    e.wait_ge(cV.h, gq)
                                for c in range(4):
                                    jb = q0 + 3 - c
                                    kc = RANKOF[jb] * TOK + SLOTOF[jb] * 128
                                    e.matmul(pSim[:, bpar * 512 + c * 128: bpar * 512 + (c + 1) * 128],
                                             kfull_s[:, kc:kc + 128], qb16_s[:, i * 128:(i + 1) * 128],
                                             start=True, stop=True)
                                fin(e, cT)
                            on("tensor", _sim)

                            def _tt(e, bpar=bpar, ct_sim=cT.n, dS_b=dS_b, ge=g_erf.get(qi - 1)):
                                e.wait_ge(cT.h, ct_sim)
                                e.wait_ge(dS.h, dS_b)
                                if ge is not None:
                                    e.wait_ge(cS_.h, ge)
                                e.tensor_tensor(out=esb_s[:, :], in0=pSim[:, bpar * 512:(bpar + 1) * 512],
                                                in1=bq_s[:, bpar * 512:(bpar + 1) * 512], op=ALU.add)
                                fin(e, cV)
                            on("vector", _tt)
                            g_simTT[qi] = cV.n

                            def _erf(e, cvn=cV.n, ga=g_aff.get(qi - 1)):
                                e.wait_ge(cV.h, cvn)
                                if ga is not None:
                                    e.wait_ge(cV.h, ga)
                                e.activation(out=ebf_s[:, :], in_=esb_s[:, :], func=AF.Erf)
                                fin(e, cS_)
                            on("scalar", _erf)
                            g_erf[qi] = cS_.n

                            def _aff(e, csn=cS_.n, bpar=bpar, gmm=g_omm.get(qi - 2)):
                                e.wait_ge(cS_.h, csn)
                                if gmm is not None:
                                    e.wait_ge(cT.h, gmm)
                                e.tensor_scalar(out=ep_s[:, bpar * 512:(bpar + 1) * 512], in0=ebf_s[:, :],
                                                scalar1=0.5, scalar2=0.5, op0=ALU.mult, op1=ALU.add)
                                fin(e, cV)
                            on("vector", _aff)
                            g_aff[qi] = cV.n

                            def _omm(e, qd=qd, bpar=bpar, op=op, par=par, cvn=cV.n, dS_v=dS_v,
                                     first=(qn == 0), last=(qn == len(quads) - 1),
                                     gp=(pout_guard.get(op) if qn == 0 else None)):
                                e.wait_ge(cV.h, cvn)
                                e.wait_ge(dS.h, dS_v)
                                if gp is not None:
                                    e.wait_ge(cV.h, gp)
                                for c in range(4):
                                    vk = par * 16 + qd * 4 + (3 - c)
                                    for h in range(2):
                                        e.matmul(pOut[:, op * 1024 + h * 512: op * 1024 + (h + 1) * 512],
                                                 ep_s[:, bpar * 512 + c * 128: bpar * 512 + (c + 1) * 128],
                                                 vt_s[:, vk * 1024 + h * 512: vk * 1024 + (h + 1) * 512],
                                                 start=(first and c == 0), stop=(last and c == 3))
                                fin(e, cT)
                            on("tensor", _omm)
                            g_omm[qi] = cT.n

                        def _acc(e, s8=s8, op=op, first_jj=first_jj, ct_out=cT.n,
                                 gtr=acc_tr_guard.get(s8)):
                            e.wait_ge(cT.h, ct_out)
                            if first_jj and gtr is not None:
                                e.wait_ge(cT.h, gtr)
                            if first_jj:
                                e.tensor_copy(out=acc_s[:, s8 * 1024:(s8 + 1) * 1024],
                                              in_=pOut[:, op * 1024:(op + 1) * 1024])
                            else:
                                e.tensor_tensor(out=acc_s[:, s8 * 1024:(s8 + 1) * 1024],
                                                in0=acc_s[:, s8 * 1024:(s8 + 1) * 1024],
                                                in1=pOut[:, op * 1024:(op + 1) * 1024], op=ALU.add)
                            fin(e, cV)
                        on("vector", _acc)
                        pout_guard[op] = cV.n

                        if last_jj and upto == 22 and i == 0:
                            def _dmpv(e, cvn=cV.n):
                                e.wait_ge(cV.h, cvn)
                                for jb in range(4):
                                    off = RANKOF[jb] * (KREG + VREG) + KREG + SLOTOF[jb] * 128 * 1024
                                    dO.inc(e.dma_start(out=ap_of(y_out, jb * 128 * TOK, [[TOK, 128], [1, 1024]]),
                                                       in_=ap_of(kvout_d, off, [[1024, 128], [1, 1024]])), 16)

                            on("gpsimd", _dmpv)
                        if last_jj and upto == 20:
                            def _dmpacc(e, s8=s8, i=i, cvn=cV.n):
                                e.wait_ge(cV.h, cvn)
                                if i < 6:
                                    dO.inc(e.dma_start(out=ap_of(y_out, i * 128 * TOK, [[TOK, 128], [1, 1024]]),
                                                       in_=acc_s[:, s8 * 1024:(s8 + 1) * 1024]), 16)
                                if i == 6:
                                    for rr in range(4):
                                        dO.inc(e.dma_start(out=ap_of(y_out, (6 * 128 + rr) * TOK, [[TOK, 1], [1, TOK]]),
                                                           in_=ap_of(f_lo_d, rr * TOK, [[TOK, 1], [1, TOK]])), 16)
                                        dO.inc(e.dma_start(out=ap_of(y_out, (6 * 128 + 8 + rr) * TOK, [[TOK, 1], [1, TOK]]),
                                                           in_=ap_of(f_hi_d, rr * TOK, [[TOK, 1], [1, TOK]])), 16)
                                    dO.inc(e.dma_start(out=ap_of(y_out, (6 * 128 + 4) * TOK, [[TOK, 1], [1, FLEN - 4 * TOK]]),
                                                       in_=ap_of(f_lo_d, 4 * TOK, [[TOK, 1], [1, FLEN - 4 * TOK]])), 16)
                                    dO.inc(e.dma_start(out=ap_of(y_out, (6 * 128 + 12) * TOK, [[TOK, 1], [1, FLEN - 4 * TOK]]),
                                                       in_=ap_of(f_hi_d, 4 * TOK, [[TOK, 1], [1, FLEN - 4 * TOK]])), 16)
                            on("gpsimd", _dmpacc)
                        if last_jj and upto not in (20, 22):
                            def _ldxc(e, i=i, cvn=cV.n):
                                e.wait_ge(cV.h, cvn)
                                dS.inc(e.dma_start(out=xcols_s[:, :], in_=ap_of(x_in, i * 128,
                                       [[TOK, 128], [128 * TOK, DC], [1, 128]])), 16)
                            on("sync", _ldxc)
                            dS_x = dS.n
                            acc_done = cV.n
                            for dc in range(DC):
                                tp = dc % 2
                                def _tr(e, s8=s8, dc=dc, tp=tp, acc_done=acc_done, gp=pT_guard.get(tp)):
                                    e.wait_ge(cV.h, acc_done)
                                    if gp is not None:
                                        e.wait_ge(cV.h, gp)
                                    e.transpose(pT[:, tp * 512:tp * 512 + 128],
                                                acc_s[:, s8 * 1024 + dc * 128: s8 * 1024 + (dc + 1) * 128],
                                                idmf_s[:, :])
                                    fin(e, cT)
                                on("tensor", _tr)

                                def _ev(e, dc=dc, tp=tp, ctn=cT.n, dS_x=dS_x, sg=stile_guard.get(dc % 2)):
                                    e.wait_ge(cT.h, ctn)
                                    e.wait_ge(dS.h, dS_x)
                                    if sg is not None:
                                        e.wait_ge(dC.h, sg)
                                    e.tensor_tensor(out=stile_s[:, (dc % 2) * 128:(dc % 2 + 1) * 128],
                                                    in0=pT[:, tp * 512:tp * 512 + 128],
                                                    in1=xcols_s[:, dc * 128:(dc + 1) * 128], op=ALU.add)
                                    fin(e, cV)
                                on("vector", _ev)
                                pT_guard[tp] = cV.n

                                def _sts(e, i=i, dc=dc, cvn=cV.n):
                                    e.wait_ge(cV.h, cvn)
                                    dC.inc(e.dma_start(out=ap_of(stream_d, dc * 128 * TOK + i * 128,
                                           [[TOK, 128], [1, 128]]), in_=stile_s[:, (dc % 2) * 128:(dc % 2 + 1) * 128]), 16)
                                on("scalar", _sts)
                                stile_guard[dc % 2] = dC.n
                            acc_tr_guard[s8] = cT.n
                    vt_guard[par] = cT.n
            barrier()
            if upto in (20, 22):
                raise _Done

    def emit_stream_to_out():
        def _cp(e):
            for dc in range(DC):
                dO.inc(e.dma_start(out=ap_of(y_out, dc * 128 * TOK, [[TOK, 128], [1, TOK]]),
                                   in_=ap_of(stream_d, dc * 128 * TOK, [[TOK, 128], [1, TOK]])), 16)
        on("gpsimd", _cp)
        barrier()

    if upto == 1:
        emit_stream_to_out()
        raise _Done

    def emit_moe(final):
        sfx = "_f" if final else "_a"
        with nc.sbuf_tensor("ewts_s" + sfx, [128, 8 * 8192], BF16) as ewts_s, \
             nc.sbuf_tensor("zspan_s" + sfx, [128, DC * SPW], F32) as zspan_s, \
             nc.sbuf_tensor("lnsp_s" + sfx, [128, DC * SPW], BF16) as lnsp_s, \
             nc.sbuf_tensor("xm_s" + sfx, [128, DC * SPW], BF16) as xm_s, \
             nc.sbuf_tensor("Ms_s" + sfx, [128, 8 * SPW], BF16) as Ms_s, \
             nc.sbuf_tensor("o1f_s" + sfx, [128, SPW], F32) as o1f_s, \
             nc.sbuf_tensor("t2_s" + sfx, [128, SPW], F32) as t2_s, \
             nc.sbuf_tensor("lgs_s" + sfx, [128, 32], F32) as lgs_s, \
             nc.sbuf_tensor("a8_s" + sfx, [128, 64], F32) as a8_s, \
             nc.sbuf_tensor("cnt_s" + sfx, [128, 8], F32) as cnt_s, \
             nc.sbuf_tensor("two_s" + sfx, [128, 8], F32) as two_s, \
             nc.sbuf_tensor("mb_s" + sfx, [128, 128], F32) as mb_s, \
             nc.psum_tensor("pD" + sfx, [128, 4096], F32) as pD:

            pS1m = pD[0:1, 0:SPW]
            pS2m = pD[0:1, 512:512 + SPW]
            pRm = pD[:, 1024:1536]
            pMRm = pD[:, 1536:2048]
            pLg = pD[:, 2048:2176]
            pMt = pD[:, 2560:2688]

            def _two(e):
                dS.inc(e.dma_start(out=two_s[:, :], in_=two_in[:, :]), 16)
            on("sync", _two)
            TWO_DS = dS.n

            def load_router(l):
                def _lr(e, l=l, ctp=cT.n):
                    e.wait_ge(cT.h, ctp)
                    dS.inc(e.dma_start(out=rwth_s[:, :], in_=ap_of(rwth_in, l * D * 128,
                           [[128, 128], [16384, 8], [1, 128]])), 16)
                    dS.inc(e.dma_start(out=rwtl_s[:, :], in_=ap_of(rwtl_in, l * D * 128,
                           [[128, 128], [16384, 8], [1, 128]])), 16)
                on("sync", _lr)
                return dS.n

            def emit_mask_ops(e, tb):
                cV.inc(e._last)
                e.wait_ge(cV.h, cV.n)
                sl = lgs_s[:, tb * 8:(tb + 1) * 8]
                in0 = sl.unsqueeze(1).broadcast_to([128, 8, 8])
                in1 = sl.unsqueeze(2).broadcast_to([128, 8, 8])
                e.tensor_tensor(out=a8_s[:, :].rearrange("p (a b) -> p a b", b=8), in0=in0, in1=in1, op=ALU.is_gt)
                cV.inc(e.tensor_reduce(out=cnt_s[:, :], in_=a8_s[:, :].rearrange("p (a b) -> p a b", b=8),
                                       axis=mybir.AxisListType.X, op=ALU.add))
                e.wait_ge(cV.h, cV.n)
                e.tensor_tensor(out=mb_s[:, 0:8], in0=cnt_s[:, :], in1=two_s[:, :], op=ALU.is_lt)

            def masks_from(zsrc, ready_ctr, ready_val, sp):
                def _hilo(e, rv=ready_val, ctp=cT.n):
                    e.wait_ge(ready_ctr.h, rv)
                    e.wait_ge(cT.h, ctp)
                    e.activation(out=xm_s[:, :], in_=zsrc[:, :], func=AF.Copy)
                    fin(e, cS_)
                on("scalar", _hilo)
                def _lo(e, csn=cS_.n):
                    e.wait_ge(cS_.h, csn)
                    e.tensor_tensor(out=Ms_s[:, 0:DC * SPW], in0=zsrc[:, :], in1=xm_s[:, :], op=ALU.subtract)
                    fin(e, cV)
                on("vector", _lo)
                HILO = cV.n
                for tb in range(4):
                    def _lg(e, tb=tb, hl=HILO, cvp=cV.n, rwl=RWLOAD):
                        e.wait_ge(cV.h, max(hl, cvp))
                        e.wait_ge(dS.h, rwl)
                        first = True
                        for dc in range(DC):
                            xh = xm_s[:, dc * SPW + tb * 128: dc * SPW + (tb + 1) * 128]
                            xl = Ms_s[:, dc * SPW + tb * 128: dc * SPW + (tb + 1) * 128]
                            wh = rwth_s[:, dc * 128:(dc + 1) * 128]
                            wl = rwtl_s[:, dc * 128:(dc + 1) * 128]
                            e.matmul(pLg[:, 0:128], xh, wh, start=first, stop=False); first = False
                            e.matmul(pLg[:, 0:128], xh, wl, start=False, stop=False)
                            e.matmul(pLg[:, 0:128], xl, wh, start=False, stop=(dc == DC - 1))
                        fin(e, cT)
                    on("tensor", _lg)

                    def _msk(e, tb=tb, ctn=cT.n, dgp=dG.n, tds=TWO_DS):
                        e.wait_ge(cT.h, ctn)
                        e.wait_ge(dS.h, tds)
                        e.wait_ge(dG.h, dgp)
                        sl = lgs_s[:, tb * 8:(tb + 1) * 8]
                        e.tensor_tensor(out=sl, in0=pLg[:, 0:8],
                                        in1=rbrep_s[:, L[0] * 8:(L[0] + 1) * 8], op=ALU.add)
                        emit_mask_ops(e, tb)
                        fin(e, cV)
                    on("vector", _msk)

                    if upto == 40 and tb == 0:
                        def _d40(e, cvn=cV.n):
                            e.wait_ge(cV.h, cvn)
                            dO.inc(e.dma_start(out=ap_of(y_out, 0, [[TOK, 128], [1, 32]]), in_=lgs_s[:, :]), 16)
                            dO.inc(e.dma_start(out=ap_of(y_out, 128 * TOK, [[TOK, 128], [1, 8]]), in_=cnt_s[:, :]), 16)
                            dO.inc(e.dma_start(out=ap_of(y_out, 256 * TOK, [[TOK, 128], [1, 8]]), in_=mb_s[:, 0:8]), 16)
                            dO.inc(e.dma_start(out=ap_of(y_out, 384 * TOK, [[TOK, 128], [1, 64]]), in_=a8_s[:, :]), 16)
                            dO.inc(e.dma_start(out=ap_of(y_out, 512 * TOK, [[TOK, 128], [1, 8]]), in_=two_s[:, :]), 16)
                        on("gpsimd", _d40)
                        barrier()
                        raise _Done
                    def _mdma(e, sp=sp, tb=tb, cvn=cV.n):
                        e.wait_ge(cV.h, cvn)
                        dG.inc(e.dma_start(out=mb_d[:, :], in_=mb_s[:, 0:8]), 16)
                        e.wait_ge(dG.h, dG.n)
                        with nc.allow_non_contiguous_dma(reason="mask transpose gather"):
                            dG.inc(e.dma_start(
                                out=maskT_s[:, sp * SPW + tb * 128: sp * SPW + (tb + 1) * 128],
                                in_=ap_of(mb_d, 0, [[1, 8], [8, 128]])), 16)
                    on("gpsimd", _mdma)

            L = [0]
            RWLOAD = load_router(0)
            for sp in range(NSPAN):
                def _ldz(e, sp=sp, csn=cS_.n, cvn=cV.n):
                    e.wait_ge(cS_.h, csn)
                    e.wait_ge(cV.h, cvn)
                    dS.inc(e.dma_start(out=zspan_s[:, :], in_=ap_of(stream_d, sp * SPW,
                           [[TOK, 128], [128 * TOK, DC], [1, SPW]])), 16)
                on("sync", _ldz)
                dzw = dS.n

                def ln_apply(e, dc):
                    e.tensor_tensor(out=t2_s[:, :], in0=zspan_s[:, dc * SPW:(dc + 1) * SPW], in1=pRm, op=ALU.mult)
                    e.tensor_tensor(out=zspan_s[:, dc * SPW:(dc + 1) * SPW], in0=t2_s[:, :], in1=pMRm, op=ALU.subtract)

                emit_dance(lambda dc: zspan_s[:, dc * SPW:(dc + 1) * SPW], False, 0.0,
                           pS1m, pS2m, pRm, pMRm, None, xm_s, Ms_s, ln_apply,
                           pre=((dS, dzw),))
                LNF = cV.n

                def _cast(e, lnf=LNF):
                    e.wait_ge(cV.h, lnf)
                    e.activation(out=lnsp_s[:, :], in_=zspan_s[:, :], func=AF.Copy)
                    fin(e, cS_)
                on("scalar", _cast)

                def _stln(e, sp=sp, csn=cS_.n):
                    e.wait_ge(cS_.h, csn)
                    dS.inc(e.dma_start(out=ap_of(lnbf_d, sp * SPW, [[TOK, 128], [128 * TOK, DC], [1, SPW]]),
                                       in_=lnsp_s[:, :]), 16)
                on("sync", _stln)

                if upto == 27 and sp == 0:
                    def _d27(e, lnf=LNF, css=cS_.n, dss=dS.n):
                        e.wait_ge(cV.h, lnf)
                        e.wait_ge(cS_.h, css)
                        e.wait_ge(dS.h, dss)
                        dO.inc(e.dma_start(out=ap_of(y_out, 0, [[TOK, 128], [1, 4096]]), in_=zspan_s[:, :]), 16)
                        dO.inc(e.dma_start(out=ap_of(y_out, 128 * TOK, [[TOK, 128], [1, 4096]]), in_=lnsp_s[:, :]), 16)
                        dO.inc(e.dma_start(out=ap_of(y_out, 256 * TOK, [[TOK, 128], [SPW, 8], [1, SPW]]),
                                           in_=ap_of(lnbf_d, 0, [[TOK, 128], [128 * TOK, 8], [1, SPW]])), 16)
                    on("gpsimd", _d27)
                    barrier()
                    raise _Done
                masks_from(zspan_s, cV, LNF, sp)

            if upto == 25:
                def _d25(e, dss=dS.n, ccc=cV.n):
                    e.wait_ge(dS.h, dss)
                    e.wait_ge(cV.h, ccc)
                    for dc in range(DC):
                        dO.inc(e.dma_start(out=ap_of(y_out, dc * 128 * TOK, [[TOK, 128], [1, TOK]]),
                                           in_=ap_of(lnbf_d, dc * 128 * TOK, [[TOK, 128], [1, TOK]])), 16)
                on("gpsimd", _d25)
                barrier()
                raise _Done

            for l in range(2):
                L[0] = l
                def _ls(e, l=l, tguard=cT.n):
                    e.wait_ge(cT.h, tguard)
                    for ee in range(8):
                        dG.inc(e.dma_start(out=ewts_s[:, ee * 8192:(ee + 1) * 8192],
                                           in_=ap_of(ewt_in, (l * 8 + ee) * D * D,
                                                     [[1024, 128], [128 * 1024, 8], [1, 1024]])), 16)
                on("gpsimd", _ls)
                slab_w = dG.n
                src_d = lnbf_d if l == 0 else lnbf2_d

                for sp in range(NSPAN):
                    def _ldl(e, sp=sp, l=l, src_d=src_d, slab_w=slab_w, cvn=cV.n):
                        e.wait_ge(dG.h, slab_w)
                        e.wait_ge(cV.h, cvn)
                        dS.inc(e.dma_start(out=lnsp_s[:, :], in_=ap_of(src_d, sp * SPW,
                               [[TOK, 128], [128 * TOK, DC], [1, SPW]])), 16)
                        if l == 1:
                            dS.inc(e.dma_start(out=zspan_s[:, :], in_=ap_of(stream_d, sp * SPW,
                                   [[TOK, 128], [128 * TOK, DC], [1, SPW]])), 16)
                    on("sync", _ldl)
                    dS_l = dS.n

                    def _mtiles(e, sp=sp, prev_evac=cV.n, cs_prev=cS_.n, dgm=dG.n):
                        e.wait_ge(cV.h, prev_evac)
                        e.wait_ge(cS_.h, cs_prev)
                        e.wait_ge(dG.h, dgm)
                        for ee in range(8):
                            e.matmul(pD[:, ee * 512:(ee + 1) * 512], sel8_s[:, ee * 128:(ee + 1) * 128],
                                     maskT_s[:, sp * SPW:(sp + 1) * SPW], start=True, stop=True)
                        fin(e, cT)
                    on("tensor", _mtiles)

                    def _mcp2(e, ctn=cT.n):
                        e.wait_ge(cT.h, ctn)
                        for ee in range(8):
                            e.tensor_copy(out=Ms_s[:, ee * SPW:(ee + 1) * SPW], in_=pD[:, ee * 512:(ee + 1) * 512])
                        fin(e, cV)
                    on("vector", _mcp2)
                    cv_m = cV.n

                    if upto == 30 and l == 0 and sp == 0:
                        def _d30(e, cvm=cv_m, dsl=dS_l):
                            e.wait_ge(cV.h, cvm)
                            e.wait_ge(dS.h, dsl)
                            dO.inc(e.dma_start(out=ap_of(y_out, 0, [[TOK, 128], [1, 32]]), in_=lgs_s[:, :]), 16)
                            dO.inc(e.dma_start(out=ap_of(y_out, 128 * TOK, [[TOK, 128], [1, 4096]]), in_=xm_s[:, :]), 16)
                            dO.inc(e.dma_start(out=ap_of(y_out, 256 * TOK, [[TOK, 8], [1, TOK]]), in_=maskT_s[:, :]), 16)
                        on("gpsimd", _d30)
                        barrier()
                        raise _Done
                    for ee in range(8):
                        def _xm(e, ee=ee, cv_m=cv_m, dS_l=dS_l, tgv=(cT.n if ee > 0 else None)):
                            e.wait_ge(cV.h, cv_m)
                            e.wait_ge(dS.h, dS_l)
                            if tgv is not None:
                                e.wait_ge(cT.h, tgv)
                            src3 = lnsp_s[:, :].rearrange("p (h t) -> p h t", h=DC)
                            msl = Ms_s[:, ee * SPW:(ee + 1) * SPW].unsqueeze(1).broadcast_to([128, DC, SPW])
                            e.tensor_tensor(out=xm_s[:, :].rearrange("p (h t) -> p h t", h=DC),
                                            in0=src3, in1=msl, op=ALU.mult)
                            fin(e, cV)
                        on("vector", _xm)

                        def _emm(e, ee=ee, cvx=cV.n):
                            e.wait_ge(cV.h, cvx)
                            for ddx in range(DC):
                                for hcc in range(DC):
                                    e.matmul(pD[:, ddx * 512:(ddx + 1) * 512],
                                             ewts_s[:, ee * 8192 + hcc * 1024 + ddx * 128:
                                                    ee * 8192 + hcc * 1024 + (ddx + 1) * 128],
                                             xm_s[:, hcc * SPW:(hcc + 1) * SPW],
                                             start=(ee == 0 and hcc == 0), stop=False)
                            fin(e, cT)
                        on("tensor", _emm)

                    def _bmm(e, l=l, sp=sp):
                        for ddx in range(DC):
                            e.matmul(pD[:, ddx * 512:(ddx + 1) * 512],
                                     ebs_s[:, l * 1024 + ddx * 128: l * 1024 + (ddx + 1) * 128],
                                     maskT_s[:, sp * SPW:(sp + 1) * SPW], start=False, stop=True)
                        fin(e, cT)
                    on("tensor", _bmm)
                    ct_exp = cT.n

                    if l == 0:
                        DDORD = [4, 0, 1, 2, 3, 5, 6, 7]
                        for di, ddx in enumerate(DDORD):
                            def _ev0(e, ddx=ddx, di=di, ct_exp=ct_exp, dsg=dS.n):
                                e.wait_ge(cT.h, ct_exp)
                                if di > 0:
                                    e.wait_ge(dS.h, dsg)
                                e.tensor_copy(out=o1f_s[:, :], in_=pD[:, ddx * 512:(ddx + 1) * 512])
                                fin(e, cV)
                            on("vector", _ev0)

                            def _cst0(e, ddx=ddx, ct_exp=ct_exp, cvn=cV.n):
                                e.wait_ge(cT.h, ct_exp)
                                e.wait_ge(cV.h, cvn)
                                e.activation(out=lnsp_s[:, ddx * SPW:(ddx + 1) * SPW],
                                             in_=pD[:, ddx * 512:(ddx + 1) * 512], func=AF.Copy)
                                fin(e, cS_)
                            on("scalar", _cst0)

                            def _sto1(e, ddx=ddx, sp=sp, cvn=cV.n):
                                e.wait_ge(cV.h, cvn)
                                dS.inc(e.dma_start(out=ap_of(o1f32_d, ddx * 128 * TOK + sp * SPW,
                                       [[TOK, 128], [1, SPW]]), in_=o1f_s[:, :]), 16)
                            on("sync", _sto1)

                        def _st0(e, sp=sp, csn=cS_.n):
                            e.wait_ge(cS_.h, csn)
                            dS.inc(e.dma_start(out=ap_of(lnbf2_d, sp * SPW, [[TOK, 128], [128 * TOK, DC], [1, SPW]]),
                                               in_=lnsp_s[:, :]), 16)
                        on("sync", _st0)

                    elif upto == 26:
                        def _d26(e, ct_exp=ct_exp):
                            e.wait_ge(cT.h, ct_exp)
                        on("tensor", _d26)
                    else:
                        def _ev1(e, ct_exp=ct_exp, dS_l=dS_l):
                            e.wait_ge(cT.h, ct_exp)
                            e.wait_ge(dS.h, dS_l)
                            for ddx in range(DC):
                                e.tensor_tensor(out=zspan_s[:, ddx * SPW:(ddx + 1) * SPW],
                                                in0=pD[:, ddx * 512:(ddx + 1) * 512],
                                                in1=zspan_s[:, ddx * SPW:(ddx + 1) * SPW], op=ALU.add)
                            fin(e, cV)
                        on("vector", _ev1)

                        dst = y_out if final else stream_d
                        ctr = dO if final else dS

                        def _st1(e, sp=sp, dst=dst, ctr=ctr, cvn=cV.n):
                            e.wait_ge(cV.h, cvn)
                            ctr.inc(e.dma_start(out=ap_of(dst, sp * SPW, [[TOK, 128], [128 * TOK, DC], [1, SPW]]),
                                                in_=zspan_s[:, :]), 16)
                        on("sync", _st1)
            barrier()

    emit_moe(final=False)
    if upto == 2:
        emit_stream_to_out()
        raise _Done

    with nc.sbuf_tensor("zsp4_s", [128, DC * SPW], F32) as zsp4_s, \
         nc.sbuf_tensor("xb4_s", [128, DC * SPW], BF16) as xb4_s, \
         nc.sbuf_tensor("sq4_s", [128, DC * SPW], BF16) as sq4_s, \
         nc.sbuf_tensor("uf4_s", [128, DC * SPW], F32) as uf4_s, \
         nc.sbuf_tensor("ur4_s", [128, DC * SPW], F32) as ur4_s, \
         nc.sbuf_tensor("t4_s", [128, SPW], F32) as t4_s, \
         nc.psum_tensor("pF", [128, 2560], F32) as pF:
        pF1 = pF[0:1, 0:SPW]
        pF2 = pF[0:1, 512:512 + SPW]
        pF3 = pF[:, 1024:1536]
        pF4 = pF[:, 1536:2048]
        pF5 = pF[:, 2048:2560]

        for sp in range(NSPAN):
            def _ldz4(e, sp=sp, csn=cS_.n, cvn=cV.n):
                e.wait_ge(cS_.h, csn)
                e.wait_ge(cV.h, cvn)
                dS.inc(e.dma_start(out=zsp4_s[:, :], in_=ap_of(stream_d, sp * SPW,
                       [[TOK, 128], [128 * TOK, DC], [1, SPW]])), 16)
            on("sync", _ldz4)
            dzw = dS.n

            def u_apply(e, dc):
                e.tensor_tensor(out=t4_s[:, :], in0=zsp4_s[:, dc * SPW:(dc + 1) * SPW], in1=pF3, op=ALU.mult)
                e.tensor_tensor(out=uf4_s[:, dc * SPW:(dc + 1) * SPW], in0=t4_s[:, :], in1=pF4, op=ALU.subtract)

            def ur_apply(e, dc):
                e.scalar_tensor_tensor(out=ur4_s[:, dc * SPW:(dc + 1) * SPW],
                                       in0=uf4_s[:, dc * SPW:(dc + 1) * SPW],
                                       scalar=ssm_s[:, 72 + dc:73 + dc], in1=pF5,
                                       op0=ALU.mult, op1=ALU.mult)

            emit_dance(lambda dc: zsp4_s[:, dc * SPW:(dc + 1) * SPW], True, EPS_RMS,
                       pF1, pF2, pF3, pF4, pF5, xb4_s, sq4_s, u_apply, ur_apply,
                       pre=((dS, dzw),))

            def _stu(e, sp=sp, cvn=cV.n):
                e.wait_ge(cV.h, cvn)
                dS.inc(e.dma_start(out=ap_of(u_d, sp * SPW, [[TOK, 128], [128 * TOK, DC], [1, SPW]]),
                                   in_=uf4_s[:, :]), 16)
                dS.inc(e.dma_start(out=ap_of(urin_d, sp * SPW, [[TOK, 128], [128 * TOK, DC], [1, SPW]]),
                                   in_=ur4_s[:, :]), 16)
            on("sync", _stu)

        def _ag2(e, dsn=dS.n):
            e.wait_ge(dS.h, dsn)
            cc.inc(e.collective_compute("AllGather", ALU.bypass, replica_groups=RG,
                                        ins=[urin_d[:, :]], outs=[urout_d[:, :]]))
        on("gpsimd", _ag2)

    def span_runs(sp4):
        runs = []
        for r in range(4):
            if sp4 < 2:
                s0, off = 8 * sp4, r
            else:
                s0, off = 16 + (sp4 - 2) * 8, 3 - r
            runs.append((r, s0, off))
        return runs

    with nc.sbuf_tensor("urs_s", [128, 2 * TOK], F32) as urs_s, \
         nc.sbuf_tensor("ss_s", [128, 2 * TOK], F32) as ss_s, \
         nc.sbuf_tensor("ys_s", [128, 2 * TOK], F32) as ys_s, \
         nc.sbuf_tensor("sst_s", [128, 32], F32) as sst_s:

        rgreg = nc.gpsimd.alloc_register("rgreg")

        def _ldrg(e):
            e.wait_ge(dS.h, P0_SMALL)
            e.load(rgreg, rgs_s[0:1, 0:1])
        on("gpsimd", _ldrg)

        def _urcp(e, ccn=cc.n):
            e.wait_ge(cc.h, ccn)
            rv = e.snap(rgreg, min_val=0, max_val=3)
            dG.inc(e.dma_start(out=ap_of(urmine_d, 0, [[256 * TOK, 4], [TOK, 256], [1, TOK]]),
                               in_=bass.AP(urout_d.tensor, rv * (256 * TOK),
                                           [[D * TOK, 4], [TOK, 256], [1, TOK]])), 16)
        on("gpsimd", _urcp)
        URCP = dG.n

        def _z(e, ccn=cc.n):
            e.wait_ge(cc.h, ccn)
            e.memset(sst_s[:, :], 0.0)
            fin(e, cV)
        on("vector", _z)

        for li in range(2):
            for sp4 in range(4):
                par = (li * 4 + sp4) % 2
                def _ldur(e, li=li, sp4=sp4, par=par, guard=(cV.n if (li * 4 + sp4) >= 2 else None), urcp=URCP):
                    e.wait_ge(dG.h, urcp)
                    if guard is not None:
                        e.wait_ge(cV.h, guard)
                    for (r, s0, off) in span_runs(sp4):
                        src_off = r * (256 * TOK) + li * (128 * TOK) + s0 * 128
                        dG.inc(e.dma_start(out=ap_of(urs_s, par * TOK + off * 128,
                                           [[2 * TOK, 128], [512, 8], [1, 128]]),
                                           in_=ap_of(urmine_d, src_off,
                                                     [[TOK, 128], [128, 8], [1, 128]])), 16)
                on("gpsimd", _ldur)
                dg_ur = dG.n

                def _scan(e, li=li, sp4=sp4, par=par, dg_ur=dg_ur):
                    e.wait_ge(dG.h, dg_ur)
                    for n in range(NST):
                        col = li * 16 + n
                        dacol = ssm_s[:, col:col + 1].broadcast_to([128, TOK])
                        init = 0.0 if sp4 == 0 else sst_s[:, col:col + 1]
                        cV.inc(e.tensor_tensor_scan(out=ss_s[:, (n % 2) * TOK:(n % 2) * TOK + TOK],
                                             data0=dacol, data1=urs_s[:, par * TOK:par * TOK + TOK],
                                             initial=init, op0=ALU.mult, op1=ALU.add))
                        e.wait_ge(cV.h, cV.n)
                        cV.inc(e.tensor_copy(out=sst_s[:, col:col + 1],
                                      in_=ss_s[:, (n % 2) * TOK + TOK - 1:(n % 2) * TOK + TOK]))
                        e.wait_ge(cV.h, cV.n)
                        if n == 0:
                            e.tensor_scalar(out=ys_s[:, 0:TOK],
                                            in0=ss_s[:, 0:TOK],
                                            scalar1=ssm_s[:, 32 + col:33 + col], scalar2=None, op0=ALU.mult)
                        else:
                            e.scalar_tensor_tensor(out=ys_s[:, (n % 2) * TOK:(n % 2) * TOK + TOK],
                                                   in0=ss_s[:, (n % 2) * TOK:(n % 2) * TOK + TOK],
                                                   scalar=ssm_s[:, 32 + col:33 + col],
                                                   in1=ys_s[:, ((n + 1) % 2) * TOK:((n + 1) % 2) * TOK + TOK],
                                                   op0=ALU.mult, op1=ALU.add)
                    fin(e, cV)
                on("vector", _scan)

                def _sty(e, li=li, sp4=sp4, cvn=cV.n):
                    e.wait_ge(cV.h, cvn)
                    for (r, s0, off) in span_runs(sp4):
                        dG.inc(e.dma_start(out=ap_of(yin_d, li * 128 * S + r * TOK + s0 * 128,
                                           [[S, 128], [128, 8], [1, 128]]),
                                           in_=ap_of(ys_s, TOK + off * 128,
                                                     [[2 * TOK, 128], [512, 8], [1, 128]])), 16)
                on("gpsimd", _sty)

        def _ag3(e, dgn=dG.n):
            e.wait_ge(dG.h, dgn)
            cc.inc(e.collective_compute("AllGather", ALU.bypass, replica_groups=RG,
                                        ins=[yin_d[:, :]], outs=[yout_d[:, :]]))
        on("gpsimd", _ag3)

    with nc.sbuf_tensor("u4_s", [128, DC * SPW], F32) as u4_s, \
         nc.sbuf_tensor("y4_s", [128, DC * SPW], F32) as y4_s, \
         nc.sbuf_tensor("zsp5_s", [128, DC * SPW], F32) as zsp5_s, \
         nc.sbuf_tensor("sq5_s", [128, DC * SPW], BF16) as sq5_s, \
         nc.sbuf_tensor("h4_s", [128, DC * SPW], BF16) as h4_s, \
         nc.sbuf_tensor("w1s_s", [128, 8 * 4096], BF16) as w1s_s, \
         nc.sbuf_tensor("w2s_s", [128, 2 * 1024], BF16) as w2s_s, \
         nc.sbuf_tensor("h1g_s", [128, 16 * 512], BF16) as h1g_s, \
         nc.sbuf_tensor("accA_s", [128, DC * SPW], F32) as accA_s, \
         nc.psum_tensor("pG", [128, 4096], F32) as pG:
        pH2 = pG[0:1, 0:SPW]
        pH3 = pG[:, 512:1024]

        def _ldw1(e):
            for hcc in range(DC):
                dG.inc(e.dma_start(out=w1s_s[:, hcc * 4096:(hcc + 1) * 4096],
                                   in_=ap_of(w1_in, hcc * 128 * DFF, [[DFF, 128], [1, 4096]])), 16)
        on("gpsimd", _ldw1)
        W1L = dG.n

        rgreg2 = nc.gpsimd.alloc_register("rgreg2")
        def _ldrg2(e):
            e.load(rgreg2, rgs_s[0:1, 0:1])
        on("gpsimd", _ldrg2)

        def _ycp(e, ccn=cc.n):
            e.wait_ge(cc.h, ccn)
            rv = e.snap(rgreg2, min_val=0, max_val=3)
            dG.inc(e.dma_start(out=ap_of(ymine_d, 0, [[TOK, 1024], [1, TOK]]),
                               in_=bass.AP(yout_d.tensor, rv * TOK, [[S, 1024], [1, TOK]])), 16)
        on("gpsimd", _ycp)
        YCP = dG.n

        for sp in range(NSPAN):
            def _ldu(e, sp=sp, ccn=cc.n, csn=cS_.n, cvn=cV.n):
                e.wait_ge(cc.h, ccn)
                e.wait_ge(cS_.h, csn)
                e.wait_ge(cV.h, cvn)
                dS.inc(e.dma_start(out=u4_s[:, :], in_=ap_of(u_d, sp * SPW,
                       [[TOK, 128], [128 * TOK, DC], [1, SPW]])), 16)
                dS.inc(e.dma_start(out=zsp5_s[:, :], in_=ap_of(stream_d, sp * SPW,
                       [[TOK, 128], [128 * TOK, DC], [1, SPW]])), 16)
            on("sync", _ldu)
            dS_u = dS.n

            def _ldy(e, sp=sp, ycp=YCP, cvn=cV.n):
                e.wait_ge(dG.h, ycp)
                e.wait_ge(cV.h, cvn)
                dG.inc(e.dma_start(out=y4_s[:, :], in_=ap_of(ymine_d, sp * SPW,
                       [[TOK, 128], [128 * TOK, DC], [1, SPW]])), 16)
            on("gpsimd", _ldy)
            dG_y = dG.n

            def _u2(e, dS_u=dS_u, dG_y=dG_y):
                e.wait_ge(dS.h, dS_u)
                e.wait_ge(dG.h, dG_y)
                for dc in range(DC):
                    e.scalar_tensor_tensor(out=y4_s[:, dc * SPW:(dc + 1) * SPW],
                                           in0=u4_s[:, dc * SPW:(dc + 1) * SPW],
                                           scalar=ssm_s[:, 64 + dc:65 + dc],
                                           in1=y4_s[:, dc * SPW:(dc + 1) * SPW], op0=ALU.mult, op1=ALU.add)
                for dc in range(DC):
                    e.tensor_tensor(out=zsp5_s[:, dc * SPW:(dc + 1) * SPW],
                                    in0=y4_s[:, dc * SPW:(dc + 1) * SPW],
                                    in1=zsp5_s[:, dc * SPW:(dc + 1) * SPW], op=ALU.add)
                fin(e, cV)
            on("vector", _u2)
            U2R = cV.n

            def _sq5(e, cvn=U2R):
                e.wait_ge(cV.h, cvn)
                for dc in range(DC):
                    e.activation(out=sq5_s[:, dc * SPW:(dc + 1) * SPW],
                                 in_=y4_s[:, dc * SPW:(dc + 1) * SPW], func=AF.Square)
                fin(e, cS_)
            on("scalar", _sq5)

            def _smm(e, csn=cS_.n, cvp=cV.n):
                e.wait_ge(cS_.h, csn)
                e.wait_ge(cV.h, cvp)
                for dc in range(DC):
                    e.matmul(pH2, onescb_s[:, :], sq5_s[:, dc * SPW:(dc + 1) * SPW],
                             start=(dc == 0), stop=(dc == DC - 1))
                fin(e, cT)
            on("tensor", _smm)

            def _rr2(e, ctn=cT.n):
                e.wait_ge(cT.h, ctn)
                e.tensor_scalar(out=SL(3), in0=pH2, scalar1=1.0 / D, scalar2=EPS_RMS, op0=ALU.mult, op1=ALU.add)
                cV.inc(e._last)
                e.wait_ge(cV.h, cV.n)
                e.reciprocal(out=SL(5), in_=SL(3))
                fin(e, cV)
            on("vector", _rr2)

            def _rr2b(e, cvn=cV.n):
                e.wait_ge(cV.h, cvn)
                e.activation(out=SL(1), in_=SL(5), func=AF.Sqrt)
                fin(e, cS_)
            on("scalar", _rr2b)

            def _rr2c(e, csn=cS_.n):
                e.wait_ge(cS_.h, csn)
                e.tensor_tensor(out=SL(4), in0=SL(1), in1=SL(1), op=ALU.mult)
                e.tensor_tensor(out=SL(4), in0=SL(4), in1=SL(3), op=ALU.mult)
                e.tensor_scalar(out=SL(4), in0=SL(4), scalar1=-0.5, scalar2=1.5, op0=ALU.mult, op1=ALU.add)
                e.tensor_tensor(out=SL(1), in0=SL(1), in1=SL(4), op=ALU.mult)
                fin(e, cV)
            on("vector", _rr2c)

            def _rb2(e, cvn=cV.n):
                e.wait_ge(cV.h, cvn)
                e.matmul(pH3, onesrf_s[:, :], SL(1), start=True, stop=True)
                fin(e, cT)
            on("tensor", _rb2)

            def _happ(e, ctn=cT.n):
                e.wait_ge(cT.h, ctn)
                for dc in range(DC):
                    e.scalar_tensor_tensor(out=h4_s[:, dc * SPW:(dc + 1) * SPW],
                                           in0=y4_s[:, dc * SPW:(dc + 1) * SPW],
                                           scalar=ssm_s[:, 80 + dc:81 + dc], in1=pH3,
                                           op0=ALU.mult, op1=ALU.mult)
                fin(e, cV)
            on("vector", _happ)
            H_READY = cV.n

            for phalf in range(2):
                gelu_at = []
                for fi in range(16):
                    fc = phalf * 16 + fi
                    pp = fi % 2

                    def _w1mm(e, fc=fc, pp=pp, fi=fi, hrd=(cV.n if (phalf == 1 and fi == 0) else H_READY), w1l=W1L,
                              gprev=(cS_.n if fi >= 2 else None)):
                        e.wait_ge(cV.h, hrd)
                        e.wait_ge(dG.h, w1l)
                        if gprev is not None:
                            e.wait_ge(cS_.h, gprev)
                        for hcc in range(DC):
                            e.matmul(pG[:, pp * 512:(pp + 1) * 512],
                                     w1s_s[:, hcc * 4096 + fc * 128: hcc * 4096 + (fc + 1) * 128],
                                     h4_s[:, hcc * SPW:(hcc + 1) * SPW],
                                     start=(hcc == 0), stop=(hcc == DC - 1))
                        fin(e, cT)
                    on("tensor", _w1mm)

                    def _gel(e, fi=fi, pp=pp, fc=fc, ctn=cT.n):
                        e.wait_ge(cT.h, ctn)
                        e.activation(out=h1g_s[:, fi * 512:(fi + 1) * 512], in_=pG[:, pp * 512:(pp + 1) * 512],
                                     func=AF.Gelu, bias=ssm_s[:, 96 + fc:97 + fc], scale=1.0)
                        fin(e, cS_)
                    on("scalar", _gel)
                    gelu_at.append(cS_.n)

                for fi in range(16):
                    fc = phalf * 16 + fi
                    wp = fi % 2

                    def _ldw2(e, fc=fc, wp=wp, fi=fi, tguard=(cT.n if fi >= 2 else None)):
                        if tguard is not None:
                            e.wait_ge(cT.h, tguard)
                        dG.inc(e.dma_start(out=w2s_s[:, wp * 1024:(wp + 1) * 1024],
                                           in_=ap_of(w2_in, fc * 128 * D, [[D, 128], [1, 1024]])), 16)
                    on("gpsimd", _ldw2)
                    dS_w2 = dG.n

                    def _w2mm(e, fi=fi, wp=wp, dS_w2=dS_w2,
                              gat=(gelu_at[15] if fi == 0 else gelu_at[fi])):
                        e.wait_ge(dG.h, dS_w2)
                        e.wait_ge(cS_.h, gat)
                        for ddx in range(DC):
                            e.matmul(pG[:, ddx * 512:(ddx + 1) * 512],
                                     w2s_s[:, wp * 1024 + ddx * 128: wp * 1024 + (ddx + 1) * 128],
                                     h1g_s[:, fi * 512:(fi + 1) * 512],
                                     start=(fi == 0), stop=(fi == 15))
                        fin(e, cT)
                    on("tensor", _w2mm)
                ct_w2 = cT.n

                if phalf == 0:
                    def _evA(e, ct_w2=ct_w2):
                        e.wait_ge(cT.h, ct_w2)
                        for ddx in range(DC):
                            e.tensor_copy(out=accA_s[:, ddx * SPW:(ddx + 1) * SPW], in_=pG[:, ddx * 512:(ddx + 1) * 512])
                        fin(e, cV)
                    on("vector", _evA)
                else:
                    def _evB(e, ct_w2=ct_w2):
                        e.wait_ge(cT.h, ct_w2)
                        for ddx in range(DC):
                            e.scalar_tensor_tensor(out=accA_s[:, ddx * SPW:(ddx + 1) * SPW],
                                                   in0=pG[:, ddx * 512:(ddx + 1) * 512],
                                                   scalar=ssm_s[:, 88 + ddx:89 + ddx],
                                                   in1=accA_s[:, ddx * SPW:(ddx + 1) * SPW],
                                                   op0=ALU.add, op1=ALU.add)
                        for ddx in range(DC):
                            e.tensor_tensor(out=zsp5_s[:, ddx * SPW:(ddx + 1) * SPW],
                                            in0=accA_s[:, ddx * SPW:(ddx + 1) * SPW],
                                            in1=zsp5_s[:, ddx * SPW:(ddx + 1) * SPW], op=ALU.add)
                        fin(e, cV)
                    on("vector", _evB)

                    def _stf(e, sp=sp, cvn=cV.n):
                        e.wait_ge(cV.h, cvn)
                        dS.inc(e.dma_start(out=ap_of(stream_d, sp * SPW, [[TOK, 128], [128 * TOK, DC], [1, SPW]]),
                                           in_=zsp5_s[:, :]), 16)
                    on("sync", _stf)
        barrier()

    if upto == 3:
        emit_stream_to_out()
        raise _Done

    emit_moe(final=True)

    def _fin(e):
        e.wait_ge(dO.h, dO.n)
    on("gpsimd", _fin)
    on("sync", _fin)

    st.close()


def kernel(L0_attn_a_x, L0_attn_a_Wqk, L0_attn_a_bqk, L0_attn_a_gamma,
           L0_attn_a_beta, L0_attn_a_Wv, L0_attn_a_bv, L0_attn_a_rel_emb,
           L0_attn_b_A_ssm, L0_attn_b_B_ssm, L0_attn_b_C_ssm, L0_attn_b_D_ssm,
           L0_attn_b_scale1, L0_attn_b_scale2, L0_attn_b_w1, L0_attn_b_b1,
           L0_attn_b_w2, L0_attn_b_b2, L0_ffn_router_w, L0_ffn_router_b,
           L0_ffn_expert_w, L0_ffn_expert_b, upto=4):
    hc = _host_constants()
    x = np.asarray(L0_attn_a_x, dtype=np.float32)
    ewT = np.ascontiguousarray(np.asarray(L0_ffn_expert_w, np.float32).transpose(0, 1, 3, 2))
    rwT = np.ascontiguousarray(np.asarray(L0_ffn_router_w, np.float32).transpose(0, 2, 1))
    rwTp = np.zeros((2, D, 128), np.float32)
    rwTp[:, :, 0:8] = rwT
    rwT_hi = rwTp.astype(ml_dtypes.bfloat16)
    rwT_lo = (rwTp - rwT_hi.astype(np.float32)).astype(ml_dtypes.bfloat16)
    A = np.asarray(L0_attn_b_A_ssm, np.float32)
    Bm = np.asarray(L0_attn_b_B_ssm, np.float32)
    Cm = np.asarray(L0_attn_b_C_ssm, np.float32)
    Dv = np.asarray(L0_attn_b_D_ssm, np.float32)

    def colsv(v):
        return np.ascontiguousarray(np.asarray(v, np.float32).reshape(DC, 128).T)

    in_maps = []
    for core in range(8):
        g, rg = core // 4, core % 4
        blocks = BLOCKS[rg]
        idx = np.concatenate([np.arange(b * TPB, (b + 1) * TPB) for b in blocks])
        x_fm = np.ascontiguousarray(x[g][idx].T)
        def Asl(M):
            return np.ascontiguousarray(np.concatenate(
                [M[256 * rg + 128 * l:256 * rg + 128 * (l + 1), :] for l in range(2)], axis=1))
        im = {
            "x_fm": x_fm,
            "L0_attn_a_Wqk": np.asarray(L0_attn_a_Wqk, np.float32),
            "L0_attn_a_bqk": np.asarray(L0_attn_a_bqk, np.float32),
            "L0_attn_a_gamma": np.asarray(L0_attn_a_gamma, np.float32),
            "L0_attn_a_beta": np.asarray(L0_attn_a_beta, np.float32),
            "L0_attn_a_Wv": np.asarray(L0_attn_a_Wv, np.float32),
            "L0_attn_a_bv": np.asarray(L0_attn_a_bv, np.float32),
            "L0_attn_a_rel_emb": np.asarray(L0_attn_a_rel_emb, np.float32),
            "oht_lo": hc["ohts"][SH_LO[rg]],
            "oht_hi": hc["ohts"][SH_HI[rg]],
            "madd_lo": hc["madds"][SH_LO[rg]],
            "madd_hi": hc["madds"][SH_HI[rg]],
            "A_mine": Asl(A), "B_mine": Asl(Bm), "C_mine": Asl(Cm),
            "D_cols": colsv(Dv), "s1_cols": colsv(L0_attn_b_scale1),
            "s2_cols": colsv(L0_attn_b_scale2), "b2_cols": colsv(L0_attn_b_b2),
            "b1_cols": np.ascontiguousarray(np.asarray(L0_attn_b_b1, np.float32).reshape(32, 128).T),
            "L0_attn_b_w1": np.asarray(L0_attn_b_w1, np.float32),
            "L0_attn_b_w2": np.asarray(L0_attn_b_w2, np.float32),
            "rwT": rwT,
            "L0_ffn_router_b": np.asarray(L0_ffn_router_b, np.float32),
            "ewT": ewT,
            "L0_ffn_expert_b": np.asarray(L0_ffn_expert_b, np.float32),
            "dtneg": hc["dtneg"],
            "idm": hc["idm"], "idmf": hc["idmf"],
            "ones_row_b": hc["ones_row_b"], "ones_row_f": hc["ones_row_f"],
            "ones_col_b": hc["ones_col_b"],
            "sel8": hc["sel8"],
            "rgv": np.array([[rg]], dtype=np.int32),
            "rb_rep": np.ascontiguousarray(np.tile(np.asarray(L0_ffn_router_b, np.float32).reshape(1, 16), (128, 1))),
            "rwT_hi": rwT_hi, "rwT_lo": rwT_lo,
            "two_rep": np.full((128, 8), 2.0, np.float32),
        }
        in_maps.append(im)

    nc = build_nc(upto)
    res = run_bass_kernel_spmd(nc, in_maps, core_ids=list(range(8)))

    out = np.empty((B, S, D), dtype=np.float32)
    for core in range(8):
        g, rg = core // 4, core % 4
        yc = np.asarray(res.results[core]["y_fm"])
        for sl, b in enumerate(BLOCKS[rg]):
            out[g, b * TPB:(b + 1) * TPB, :] = yc[:, sl * TPB:(sl + 1) * TPB].T
    return out


if __name__ == "__main__":
    import sys
    u = int(sys.argv[1]) if len(sys.argv) > 1 else 4
    nc = build_nc(u)
    n = sum(1 for _ in nc.all_instructions())
    print(f"built upto={u}, {n} instructions")
